# version 01 (final)
# baseline (speedup 1.0000x reference)
import numpy as np
import concourse.bass as bass
import concourse.mybir as mybir
from concourse.bass_utils import run_bass_kernel_spmd

F32 = mybir.dt.float32
BF16 = mybir.dt.bfloat16
AF = mybir.ActivationFunctionType
ALU = mybir.AluOpType
ET = mybir.EngineType

T = 8192
D = 2048
H = 16
DQ = 128
NC = 8
TPC = T // NC
SCALE = float(DQ) ** -0.5
EPS = 1e-5
CH = 512
NCH = T // CH


def dyn3(tile3, i):
    base = tile3[:, :, 0:1]
    off = base.offset
    try:
        noff = off + i
    except TypeError:
        noff = i + off
    return bass.AP(base.tensor, noff, base.ap)


def dyncol(tile_ap, i, width=1):
    base = tile_ap[:, 0:width]
    off = base.offset
    try:
        noff = off + i
    except TypeError:
        noff = i + off
    return bass.AP(base.tensor, noff, base.ap)


def build_A(NT):
    nc = bass.Bass("TRN2", target_bir_lowering=False)
    xT = nc.dram_tensor("xT", [D, NT], F32, kind="ExternalInput").ap()
    WqT = nc.dram_tensor("WqT", [D, D], F32, kind="ExternalInput").ap()
    WkT = nc.dram_tensor("WkT", [D, D], F32, kind="ExternalInput").ap()
    WvT = nc.dram_tensor("WvT", [D, D], F32, kind="ExternalInput").ap()
    WaT = nc.dram_tensor("WaT", [D, D], F32, kind="ExternalInput").ap()
    WgT = nc.dram_tensor("WgT", [D, D], F32, kind="ExternalInput").ap()
    WbT = nc.dram_tensor("WbT", [D, H], F32, kind="ExternalInput").ap()
    ba_r = nc.dram_tensor("ba_r", [128, 16], F32, kind="ExternalInput").ap()
    bb_r = nc.dram_tensor("bb_r", [H, 1], F32, kind="ExternalInput").ap()
    ones_c = nc.dram_tensor("ones_c", [128, 1], F32, kind="ExternalInput").ap()
    ones_r = nc.dram_tensor("ones_r", [1, 128], F32, kind="ExternalInput").ap()
    outs = {}
    for nm in ["qpre", "kpre", "vpre", "a_o", "g_o"]:
        outs[nm] = nc.dram_tensor(nm, [D, NT], F32, kind="ExternalOutput").ap()
    outs["beta_o"] = nc.dram_tensor("beta_o", [H, NT], F32, kind="ExternalOutput").ap()

    Tt = 512
    ctx = []

    def sb(name, shape, dt=F32):
        t = nc.sbuf_tensor(name, shape, dt)
        ctx.append(t)
        return t.__enter__()[:]

    def ps(name, shape, dt=F32):
        t = nc.psum_tensor(name, shape, dt)
        ctx.append(t)
        return t.__enter__()[:]

    x_sb = [sb(f"x{ic}", [128, Tt]) for ic in range(16)]
    xn_bf = [sb(f"xn{ic}", [128, Tt], BF16) for ic in range(16)]
    xbf = sb("xbf", [128, Tt], BF16)
    sq = sb("sq", [128, Tt], BF16)
    onesc_bf = sb("onescb", [128, 1], BF16)
    onesc = sb("onesc", [128, 1])
    onesr = sb("onesr", [1, 128])
    mean = sb("mean", [1, Tt])
    rstd = sb("rstd", [1, Tt])
    tmp1 = sb("tmp1", [1, Tt])
    eps_t = sb("eps", [1, 1])
    Wf = sb("Wf", [128, D])
    Wb_ = [sb(f"Wb{ic}", [128, D], BF16) for ic in range(16)]
    WbTf = sb("WbTf", [128, H])
    WbT_bf = [sb(f"WbTb{ic}", [128, H], BF16) for ic in range(16)]
    ba_sb = sb("ba_sb", [128, 16])
    bb_sb = sb("bb_sb", [H, 1])
    evac = [sb(f"ev{i}", [128, Tt]) for i in range(2)]
    evb = sb("evb", [H, Tt])
    ps_s = ps("ps_s", [1, Tt])
    ps_s2 = ps("ps_s2", [1, Tt])
    ps_b = [ps(f"ps_b{i}", [128, Tt]) for i in range(2)]
    ps_p = ps("ps_p", [128, Tt])
    ps_bt = ps("ps_bt", [H, Tt])

    with nc.semaphore("ld") as ld, nc.semaphore("st") as st:
        nld = [0]
        nst = [0]

        def dma(out, in_):
            nc.sync.dma_start(out=out, in_=in_).then_inc(ld, 16)
            nld[0] += 1

        def wld():
            nc.vector.wait_ge(ld, 16 * nld[0])
            nc.all_engine_barrier()

        def dmao(out, in_):
            nc.gpsimd.dma_start(out=out, in_=in_).then_inc(st, 16)
            nst[0] += 1

        dma(onesc, ones_c)
        dma(onesr, ones_r)
        dma(ba_sb, ba_r)
        dma(bb_sb, bb_r)
        wld()
        nc.vector.memset(eps_t, EPS)
        nc.vector.tensor_copy(onesc_bf, onesc)
        for ic in range(16):
            dma(WbTf, WbT[ic * 128:(ic + 1) * 128, :])
            wld()
            nc.vector.tensor_copy(WbT_bf[ic], WbTf)
            nc.all_engine_barrier()

        for tt in range(NT // Tt):
            c0 = tt * Tt
            for ic in range(16):
                dma(x_sb[ic], xT[ic * 128:(ic + 1) * 128, c0:c0 + Tt])
            wld()
            for ic in range(16):
                nc.vector.tensor_copy(xbf, x_sb[ic])
                nc.all_engine_barrier()
                nc.tensor.matmul(ps_s[0:1, :], onesc_bf, xbf,
                                 start=(ic == 0), stop=(ic == 15))
                nc.scalar.activation(sq, xbf, AF.Square)
                nc.all_engine_barrier()
                nc.tensor.matmul(ps_s2[0:1, :], onesc_bf, sq,
                                 start=(ic == 0), stop=(ic == 15))
                nc.all_engine_barrier()
            nc.vector.tensor_scalar_mul(mean[0:1, :], ps_s[0:1, :], 1.0 / D)
            nc.vector.tensor_scalar_mul(tmp1[0:1, :], ps_s2[0:1, :], 1.0 / D)
            nc.all_engine_barrier()
            nc.scalar.activation(rstd[0:1, :], mean[0:1, :], AF.Square)
            nc.all_engine_barrier()
            nc.vector.tensor_sub(tmp1[0:1, :], tmp1[0:1, :], rstd[0:1, :])
            nc.all_engine_barrier()
            nc.scalar.activation(tmp1[0:1, :], tmp1[0:1, :], AF.Sqrt, bias=eps_t[0:1, 0:1])
            nc.all_engine_barrier()
            nc.vector.reciprocal(rstd[0:1, :], tmp1[0:1, :])
            nc.all_engine_barrier()
            nc.tensor.matmul(ps_b[0], onesr, mean[0:1, :], start=True, stop=True)
            nc.tensor.matmul(ps_b[1], onesr, rstd[0:1, :], start=True, stop=True)
            nc.all_engine_barrier()
            for ic in range(16):
                nc.vector.tensor_sub(x_sb[ic], x_sb[ic], ps_b[0])
                nc.all_engine_barrier()
                nc.vector.tensor_mul(xn_bf[ic], x_sb[ic], ps_b[1])
                nc.all_engine_barrier()
            projs = [("q", WqT, outs["qpre"], None), ("k", WkT, outs["kpre"], None),
                     ("v", WvT, outs["vpre"], None), ("a", WaT, outs["a_o"], "ba"),
                     ("g", WgT, outs["g_o"], "sig")]
            for nm, WT, oap, kind in projs:
                for ic in range(16):
                    dma(Wf, WT[ic * 128:(ic + 1) * 128, :])
                    wld()
                    nc.vector.tensor_copy(Wb_[ic], Wf)
                    nc.all_engine_barrier()
                for oc in range(16):
                    for ic in range(16):
                        nc.tensor.matmul(ps_p, Wb_[ic][:, oc * 128:(oc + 1) * 128],
                                         xn_bf[ic], start=(ic == 0), stop=(ic == 15))
                    nc.all_engine_barrier()
                    ev = evac[oc % 2]
                    if oc >= 2:
                        nc.vector.wait_ge(st, 16 * (nst[0] - 1))
                        nc.all_engine_barrier()
                    if kind == "ba":
                        nc.scalar.activation(ev, ps_p, AF.Sigmoid,
                                             bias=ba_sb[:, oc:oc + 1])
                    elif kind == "sig":
                        nc.scalar.activation(ev, ps_p, AF.Sigmoid)
                    else:
                        nc.vector.tensor_copy(ev, ps_p)
                    nc.all_engine_barrier()
                    dmao(oap[oc * 128:(oc + 1) * 128, c0:c0 + Tt], ev)
                nc.all_engine_barrier()
            for ic in range(16):
                nc.tensor.matmul(ps_bt, WbT_bf[ic], xn_bf[ic],
                                 start=(ic == 0), stop=(ic == 15))
            nc.all_engine_barrier()
            nc.vector.tensor_scalar_add(evb, ps_bt, bb_sb[:, 0:1])
            nc.all_engine_barrier()
            nc.scalar.activation(evb, evb, AF.Sigmoid)
            nc.all_engine_barrier()
            dmao(outs["beta_o"][:, c0:c0 + Tt], evb)
            nc.all_engine_barrier()
        nc.vector.wait_ge(st, 16 * nst[0])
        nc.all_engine_barrier()
    for t_ in reversed(ctx):
        t_.__exit__(None, None, None)
    return nc


def build_B():
    nc = bass.Bass("TRN2", target_bir_lowering=False)
    qp = nc.dram_tensor("qp", [256, T + 3], F32, kind="ExternalInput").ap()
    kp = nc.dram_tensor("kp", [256, T + 3], F32, kind="ExternalInput").ap()
    vp = nc.dram_tensor("vp", [256, T + 3], F32, kind="ExternalInput").ap()
    ah = nc.dram_tensor("ah", [256, T], F32, kind="ExternalInput").ap()
    bet = nc.dram_tensor("bet", [2, T], F32, kind="ExternalInput").ap()
    qcw = nc.dram_tensor("qcw", [256, 4], F32, kind="ExternalInput").ap()
    kcw = nc.dram_tensor("kcw", [256, 4], F32, kind="ExternalInput").ap()
    vcw = nc.dram_tensor("vcw", [256, 4], F32, kind="ExternalInput").ap()
    qcb = nc.dram_tensor("qcb", [256, 1], F32, kind="ExternalInput").ap()
    kcb = nc.dram_tensor("kcb", [256, 1], F32, kind="ExternalInput").ap()
    vcb = nc.dram_tensor("vcb", [256, 1], F32, kind="ExternalInput").ap()
    lnw = nc.dram_tensor("lnw", [128, 1], F32, kind="ExternalInput").ap()
    lnb = nc.dram_tensor("lnb", [128, 1], F32, kind="ExternalInput").ap()
    ident = nc.dram_tensor("ident", [128, 128], F32, kind="ExternalInput").ap()
    ones_c = nc.dram_tensor("ones_c", [128, 1], F32, kind="ExternalInput").ap()
    ones_r = nc.dram_tensor("ones_r", [1, 128], F32, kind="ExternalInput").ap()
    o_h = nc.dram_tensor("o_h", [256, T], F32, kind="ExternalOutput").ap()

    ctx = []

    def sb(name, shape, dt=F32):
        t = nc.sbuf_tensor(name, shape, dt)
        ctx.append(t)
        return t.__enter__()[:]

    def ps(name, shape, dt=F32):
        t = nc.psum_tensor(name, shape, dt)
        ctx.append(t)
        return t.__enter__()[:]

    id_sb = sb("id", [128, 128])
    onesc = sb("onesc", [128, 1])
    onesr = sb("onesr", [1, 128])
    lnw_sb = sb("lnw_s", [128, 1])
    lnb_sb = sb("lnb_s", [128, 1])
    eps_t = sb("eps", [1, 1])
    beta_sb = [sb(f"beta{h}", [1, T]) for h in range(2)]
    cw = {}
    cb = {}
    for nm, w, b in [("q", qcw, qcb), ("k", kcw, kcb), ("v", vcw, vcb)]:
        cw[nm] = [sb(f"{nm}cw{h}", [128, 4]) for h in range(2)]
        cb[nm] = [sb(f"{nm}cb{h}", [128, 1]) for h in range(2)]
    pre = {nm: [sb(f"{nm}pre{h}", [128, CH + 3]) for h in range(2)]
           for nm in ["q", "k", "v"]}
    a_sb = [sb(f"a{h}", [128, CH]) for h in range(2)]
    kqi = [sb(f"kqi{h}", [128, 2, CH]) for h in range(2)]
    vh_ = [sb(f"vh{h}", [128, CH]) for h in range(2)]
    tmpa = sb("tmpa", [128, CH])
    tmpb = sb("tmpb", [128, CH])
    kqb = [sb(f"kqb{h}", [128, CH]) for h in range(2)]
    kt_ = [sb(f"kt{h}", [128, CH]) for h in range(2)]
    Ust = [sb(f"Ust{h}", [128, CH]) for h in range(2)]
    bb_ = [sb(f"bb{h}", [128, CH]) for h in range(2)]
    S = [sb(f"S{h}", [128, 128]) for h in range(2)]
    u1 = [sb(f"u1{h}", [128, 1]) for h in range(2)]
    ac_ = [sb(f"ac{h}", [128, 1]) for h in range(2)]
    kc_ = [sb(f"kc{h}", [128, 1]) for h in range(2)]
    u_ = [sb(f"u{h}", [128, 1]) for h in range(2)]
    o1 = [sb(f"o1{h}", [128, 1]) for h in range(2)]
    o2 = [sb(f"o2{h}", [128, 1]) for h in range(2)]
    ur = [sb(f"ur{h}", [1, 128]) for h in range(2)]
    ar = [sb(f"ar{h}", [1, 128]) for h in range(2)]
    kr = [sb(f"kr{h}", [1, 128]) for h in range(2)]
    ost = [sb(f"ost{h}", [128, CH]) for h in range(2)]
    row1 = sb("row1", [1, CH])
    ps_skq_t = ps("pskq", [128, 4])
    ps_skq = [ps_skq_t[:, 2 * h:2 * h + 2] for h in range(2)]
    ps_r_t = ps("psr", [1, 768])
    ps_r = [ps_r_t[0:1, 128 * j:128 * (j + 1)] for j in range(6)]
    ps_afou = ps("psafou", [128, 512])
    ps_af = [ps_afou[:, 128 * h:128 * (h + 1)] for h in range(2)]
    ps_ou = [ps_afou[:, 256 + 128 * h:256 + 128 * (h + 1)] for h in range(2)]
    ps_row = ps("ps_row", [1, CH])
    ps_bc = ps("ps_bc", [128, CH])
    ps_bc2 = ps("ps_bc2", [128, CH])

    ctx_go = [None]
    ctx_dummy = [sb("dummy_sb", [1, 1])]
    with nc.semaphore("ld") as ld, nc.semaphore("st") as st, nc.semaphore("go") as go_s:
        ctx_go[0] = go_s
        nld = [0]
        nst = [0]

        def dma(out, in_):
            nc.sync.dma_start(out=out, in_=in_).then_inc(ld, 16)
            nld[0] += 1

        def wld():
            nc.vector.wait_ge(ld, 16 * nld[0])
            nc.all_engine_barrier()

        def dmao(out, in_):
            nc.gpsimd.dma_start(out=out, in_=in_).then_inc(st, 16)
            nst[0] += 1

        dma(id_sb, ident)
        dma(onesc, ones_c)
        dma(onesr, ones_r)
        dma(lnw_sb, lnw)
        dma(lnb_sb, lnb)
        for h in range(2):
            dma(beta_sb[h], bet[h:h + 1, :])
            for nm, w, b in [("q", qcw, qcb), ("k", kcw, kcb), ("v", vcw, vcb)]:
                dma(cw[nm][h], w[h * 128:(h + 1) * 128, :])
                dma(cb[nm][h], b[h * 128:(h + 1) * 128, :])
        wld()
        nc.vector.memset(eps_t, EPS)
        for h in range(2):
            nc.vector.memset(S[h], 0.0)
        nc.all_engine_barrier()

        engs = [ET.PE, ET.DVE, ET.Activation, ET.Pool]
        iregs = nc.alloc_registers("scan_i", engines=engs)
        i_v = nc.snap(iregs, engines=engs, donate=True, min_val=0, max_val=CH - 1)
        d_kq2 = [dyn3(kqi[h], i_v) for h in range(2)]
        d_v = [dyncol(vh_[h], i_v) for h in range(2)]
        d_a = [dyncol(a_sb[h], i_v) for h in range(2)]
        d_kt = [dyncol(kt_[h], i_v) for h in range(2)]
        d_us = [dyncol(Ust[h], i_v) for h in range(2)]
        d_o = [dyncol(ost[h], i_v) for h in range(2)]

        def scanloop():
            lid = nc.next_id()
            ls, le = f"sl{lid}", f"sl{lid}e"
            nc.regs_mov(iregs, 0)
            nc.br(ls, engines=engs)
            with nc.body(ls, valid_engines=engs):
                scan_body()
                nc.regs_alu(iregs, iregs, 1, op=ALU.add)
                nc.br_lt(iregs, CH, on_true=ls, on_false=le, engines=engs)
            nc.switch_bb(le)

        def scan_body():
            for h in range(2):
                nc.tensor.matmul(ps_skq[h], S[h], d_kq2[h], start=True, stop=True)
            nc.multi_engine_barrier(engs)
            for h in range(2):
                nc.vector.tensor_sub(u1[h], ps_skq[h][:, 0:1], d_v[h])
                nc.scalar.activation(ac_[h], d_a[h], AF.Copy)
                nc.scalar.activation(kc_[h], d_kt[h], AF.Copy)
            nc.multi_engine_barrier(engs)
            for h in range(2):
                nc.tensor.transpose(ps_r[3 * h + 0], u1[h], id_sb)
                nc.tensor.transpose(ps_r[3 * h + 1], ac_[h], id_sb)
                nc.tensor.transpose(ps_r[3 * h + 2], kc_[h], id_sb)
                nc.vector.tensor_mul(o1[h], ps_skq[h][:, 1:2], ac_[h])
                nc.gpsimd.tensor_copy(d_us[h], u1[h])
            nc.multi_engine_barrier(engs)
            for h in range(2):
                nc.vector.tensor_copy(ur[h], ps_r[3 * h + 0])
                nc.vector.tensor_copy(ar[h], ps_r[3 * h + 1])
                nc.vector.tensor_copy(kr[h], ps_r[3 * h + 2])
                nc.gpsimd.tensor_copy(d_o[h], o1[h])
            nc.multi_engine_barrier(engs)
            for h in range(2):
                nc.tensor.matmul(ps_af[h], onesr, ar[h], start=True, stop=True)
                nc.tensor.matmul(ps_ou[h], kr[h], ur[h], start=True, stop=True)
            nc.multi_engine_barrier(engs)
            for h in range(2):
                nc.vector.tensor_mul(S[h], S[h], ps_af[h])
            for h in range(2):
                nc.vector.tensor_sub(S[h], S[h], ps_ou[h])
            nc.multi_engine_barrier(engs)

        W0 = 16 * nld[0]
        L = 8
        ST = 2
        jregs = nc.alloc_registers("chunk_j", engines=engs)
        j_v = nc.snap(jregs, engines=engs, donate=True, min_val=0, max_val=NCH - 1)
        wv = nc.alloc_registers("wv", engines=[ET.DVE])
        jc = nc.alloc_registers("jc", engines=[ET.PE])
        jc_v = nc.snap(jc, engines=[ET.PE], donate=True, min_val=0, max_val=T - CH)
        d_bet = [dyncol(beta_sb[h], jc_v, CH) for h in range(2)]
        go = ctx_go[0]
        dummy = ctx_dummy[0]

        def meb():
            nc.multi_engine_barrier(engs)

        for h in range(2):
            for nm, srcap in [("q", qp), ("k", kp), ("v", vp)]:
                nc.sync.dma_start(out=pre[nm][h], in_=srcap[h * 128:(h + 1) * 128, 0:CH + 3]).then_inc(ld, 16)
            nc.sync.dma_start(out=a_sb[h], in_=ah[h * 128:(h + 1) * 128, 0:CH]).then_inc(ld, 16)
        for j in range(NCH):
            nc.sync.wait_ge(go, j + 1)
            for h in range(2):
                nc.sync.dma_start(out=o_h[h * 128:(h + 1) * 128, j * CH:(j + 1) * CH], in_=ost[h]).then_inc(st, 16)
            nc.sync.wait_ge(st, 16 * ST * (j + 1))
            if j < NCH - 1:
                t0 = (j + 1) * CH
                for h in range(2):
                    for nm, srcap in [("q", qp), ("k", kp), ("v", vp)]:
                        nc.sync.dma_start(out=pre[nm][h], in_=srcap[h * 128:(h + 1) * 128, t0:t0 + CH + 3]).then_inc(ld, 16)
                    nc.sync.dma_start(out=a_sb[h], in_=ah[h * 128:(h + 1) * 128, t0:t0 + CH]).then_inc(ld, 16)

        lid = nc.next_id()
        ls, le = f"cl{lid}", f"cl{lid}e"
        nc.regs_mov(jregs, 0)
        nc.br(ls, engines=engs)
        with nc.body(ls, valid_engines=engs):
            nc.regs_alu(jc, jregs, CH, op=ALU.mult)
            nc.regs_alu(wv, jregs, 16 * L, op=ALU.mult)
            nc.regs_alu(wv, wv, W0 + 16 * L, op=ALU.add)
            nc.vector.wait_ge(ld, wv)
            meb()
            for h in range(2):
                for nm in ["q", "k", "v"]:
                    p = pre[nm][h]
                    nc.vector.tensor_scalar_mul(tmpa, p[:, 0:CH], cw[nm][h][:, 0:1])
                    meb()
                    for tau in range(1, 4):
                        nc.vector.tensor_scalar_mul(tmpb, p[:, tau:tau + CH], cw[nm][h][:, tau:tau + 1])
                        meb()
                        nc.vector.tensor_add(tmpa, tmpa, tmpb)
                        meb()
                    if nm == "q":
                        nc.scalar.activation(kqi[h][:, 1, :], tmpa, AF.Silu, bias=cb[nm][h][:, 0:1])
                    elif nm == "k":
                        nc.scalar.activation(tmpb, tmpa, AF.Silu, bias=cb[nm][h][:, 0:1])
                        meb()
                        nc.vector.tensor_scalar_mul(kqi[h][:, 0, :], tmpb, SCALE)
                    else:
                        nc.scalar.activation(vh_[h], tmpa, AF.Silu, bias=cb[nm][h][:, 0:1])
                    meb()
                nc.tensor.matmul(ps_bc2, onesr, d_bet[h], start=True, stop=True)
                meb()
                nc.vector.tensor_copy(bb_[h], ps_bc2)
                meb()
                nc.vector.tensor_mul(kt_[h], kqi[h][:, 0, :], bb_[h])
                meb()
                nc.vector.tensor_mul(tmpa, kt_[h], kqi[h][:, 1, :])
                meb()
                nc.tensor.matmul(ps_row, onesc, tmpa, start=True, stop=True)
                meb()
                nc.vector.tensor_copy(row1, ps_row)
                meb()
                nc.tensor.matmul(ps_bc, onesr, row1, start=True, stop=True)
                meb()
                nc.vector.tensor_copy(kqb[h], ps_bc)
                meb()
            scanloop()
            meb()
            for h in range(2):
                nc.vector.tensor_mul(tmpa, Ust[h], kqb[h])
                meb()
                nc.vector.tensor_sub(ost[h], ost[h], tmpa)
                meb()
            for h in range(2):
                nc.tensor.matmul(ps_row, onesc, ost[h], start=True, stop=True)
                meb()
                nc.vector.tensor_scalar_mul(row1, ps_row, 1.0 / 128.0)
                meb()
                nc.scalar.activation(tmpa, ost[h], AF.Square)
                meb()
                nc.tensor.matmul(ps_row, onesc, tmpa, start=True, stop=True)
                meb()
                nc.vector.tensor_scalar_mul(tmpb[0:1, :], ps_row, 1.0 / 128.0)
                meb()
                nc.scalar.activation(tmpa[0:1, :], row1, AF.Square)
                meb()
                nc.vector.tensor_sub(tmpb[0:1, :], tmpb[0:1, :], tmpa[0:1, :])
                meb()
                nc.scalar.activation(tmpb[0:1, :], tmpb[0:1, :], AF.Sqrt, bias=eps_t[0:1, 0:1])
                meb()
                nc.vector.reciprocal(tmpb[0:1, :], tmpb[0:1, :])
                meb()
                nc.tensor.matmul(ps_bc, onesr, row1, start=True, stop=True)
                nc.tensor.matmul(ps_bc2, onesr, tmpb[0:1, :], start=True, stop=True)
                meb()
                nc.vector.tensor_sub(ost[h], ost[h], ps_bc)
                meb()
                nc.vector.tensor_mul(ost[h], ost[h], ps_bc2)
                meb()
                nc.vector.tensor_scalar(ost[h], ost[h], lnw_sb[:, 0:1], lnb_sb[:, 0:1],
                                        ALU.mult, ALU.add)
                meb()
            nc.vector.memset(dummy, 0.0).then_inc(go, 1)
            nc.regs_alu(jregs, jregs, 1, op=ALU.add)
            nc.br_lt(jregs, NCH, on_true=ls, on_false=le, engines=engs)
        nc.switch_bb(le)
        nc.sync.wait_ge(st, 16 * ST * NCH)
        nc.all_engine_barrier()
    for t_ in reversed(ctx):
        t_.__exit__(None, None, None)
    return nc


def build_C(NT):
    INTER = 5632
    NI = INTER // 128
    nc = bass.Bass("TRN2", target_bir_lowering=False)
    xT = nc.dram_tensor("xT", [D, NT], F32, kind="ExternalInput").ap()
    oT = nc.dram_tensor("oT", [D, NT], F32, kind="ExternalInput").ap()
    gT = nc.dram_tensor("gT", [D, NT], F32, kind="ExternalInput").ap()
    WoT = nc.dram_tensor("WoT", [D, D], F32, kind="ExternalInput").ap()
    gwT = nc.dram_tensor("gwT", [NI, D, 128], F32, kind="ExternalInput").ap()
    uwT = nc.dram_tensor("uwT", [NI, D, 128], F32, kind="ExternalInput").ap()
    dwT = nc.dram_tensor("dwT", [NI, 128, D], F32, kind="ExternalInput").ap()
    gs_r = nc.dram_tensor("gs_r", [128, NI], F32, kind="ExternalInput").ap()
    us_r = nc.dram_tensor("us_r", [128, NI], F32, kind="ExternalInput").ap()
    ds_r = nc.dram_tensor("ds_r", [128, 16], F32, kind="ExternalInput").ap()
    ones_c = nc.dram_tensor("ones_c", [128, 1], F32, kind="ExternalInput").ap()
    ones_r = nc.dram_tensor("ones_r", [1, 128], F32, kind="ExternalInput").ap()
    outT = nc.dram_tensor("outT", [D, NT], F32, kind="ExternalOutput").ap()

    Tt = 512
    ctx = []

    def sb(name, shape, dt=F32):
        t = nc.sbuf_tensor(name, shape, dt)
        ctx.append(t)
        return t.__enter__()[:]

    def ps(name, shape, dt=F32):
        t = nc.psum_tensor(name, shape, dt)
        ctx.append(t)
        return t.__enter__()[:]

    onesc = sb("onesc", [128, 1])
    onescb = sb("onescb", [128, 1], BF16)
    onesr = sb("onesr", [1, 128])
    eps_t = sb("eps", [1, 1])
    r1 = [sb(f"r1_{ic}", [128, Tt]) for ic in range(16)]
    og_bf = [sb(f"og{ic}", [128, Tt], BF16) for ic in range(16)]
    att = [sb(f"att{ic}", [128, Tt]) for ic in range(16)]
    xn2 = [sb(f"xn2_{ic}", [128, Tt], BF16) for ic in range(16)]
    hall = [sb(f"h{m}", [128, Tt], BF16) for m in range(NI)]
    ffn = [sb(f"ffn{ic}", [128, Tt]) for ic in range(16)]
    W2f = [sb(f"W2f{ic}", [128, 128]) for ic in range(16)]
    W2b = [sb(f"W2b{ic}", [128, 128], BF16) for ic in range(16)]
    gs_sb = sb("gs", [128, NI])
    us_sb = sb("us", [128, NI])
    ds_sb = sb("ds", [128, 16])
    sq = sb("sqc", [128, Tt], BF16)
    abf = sb("abf", [128, Tt], BF16)
    mean = sb("mean", [1, Tt])
    rstd = sb("rstd", [1, Tt])
    tmp1 = sb("tmp1", [1, Tt])
    tmpc = sb("tmpc", [128, Tt])
    evac = [sb(f"ev{i}", [128, Tt]) for i in range(2)]
    ps_s = ps("ps_s", [1, Tt])
    ps_s2 = ps("ps_s2", [1, Tt])
    ps_b = [ps(f"ps_b{i}", [128, Tt]) for i in range(2)]
    ps_p = ps("ps_p", [128, Tt])
    ps_u = ps("ps_u", [128, Tt])

    def lnfull(chunks_src, out_chunks, out_dt_note, add_to=None):
        for ic in range(16):
            nc.vector.tensor_copy(abf, chunks_src[ic])
            nc.all_engine_barrier()
            nc.tensor.matmul(ps_s[0:1, :], onescb, abf, start=(ic == 0), stop=(ic == 15))
            nc.scalar.activation(sq, abf, AF.Square)
            nc.all_engine_barrier()
            nc.tensor.matmul(ps_s2[0:1, :], onescb, sq, start=(ic == 0), stop=(ic == 15))
            nc.all_engine_barrier()
        nc.vector.tensor_scalar_mul(mean[0:1, :], ps_s[0:1, :], 1.0 / D)
        nc.vector.tensor_scalar_mul(tmp1[0:1, :], ps_s2[0:1, :], 1.0 / D)
        nc.all_engine_barrier()
        nc.scalar.activation(rstd[0:1, :], mean[0:1, :], AF.Square)
        nc.all_engine_barrier()
        nc.vector.tensor_sub(tmp1[0:1, :], tmp1[0:1, :], rstd[0:1, :])
        nc.all_engine_barrier()
        nc.scalar.activation(tmp1[0:1, :], tmp1[0:1, :], AF.Sqrt, bias=eps_t[0:1, 0:1])
        nc.all_engine_barrier()
        nc.vector.reciprocal(rstd[0:1, :], tmp1[0:1, :])
        nc.all_engine_barrier()
        nc.tensor.matmul(ps_b[0], onesr, mean[0:1, :], start=True, stop=True)
        nc.tensor.matmul(ps_b[1], onesr, rstd[0:1, :], start=True, stop=True)
        nc.all_engine_barrier()
        for ic in range(16):
            nc.vector.tensor_sub(tmpc, chunks_src[ic], ps_b[0])
            nc.all_engine_barrier()
            nc.vector.tensor_mul(tmpc, tmpc, ps_b[1])
            nc.all_engine_barrier()
            if add_to is not None:
                nc.vector.tensor_add(out_chunks[ic], tmpc, add_to[ic])
            else:
                nc.vector.tensor_copy(out_chunks[ic], tmpc)
            nc.all_engine_barrier()

    with nc.semaphore("ld") as ld, nc.semaphore("st") as st:
        nld = [0]
        nst = [0]

        def dma(out, in_):
            nc.sync.dma_start(out=out, in_=in_).then_inc(ld, 16)
            nld[0] += 1

        def wld():
            nc.vector.wait_ge(ld, 16 * nld[0])
            nc.all_engine_barrier()

        def dmao(out, in_):
            nc.gpsimd.dma_start(out=out, in_=in_).then_inc(st, 16)
            nst[0] += 1

        dma(onesc, ones_c)
        dma(onesr, ones_r)
        dma(gs_sb, gs_r)
        dma(us_sb, us_r)
        dma(ds_sb, ds_r)
        wld()
        nc.vector.memset(eps_t, EPS)
        nc.vector.tensor_copy(onescb, onesc)
        nc.all_engine_barrier()

        for tt in range(NT // Tt):
            c0 = tt * Tt
            for ic in range(16):
                dma(r1[ic], xT[ic * 128:(ic + 1) * 128, c0:c0 + Tt])
                dma(att[ic], oT[ic * 128:(ic + 1) * 128, c0:c0 + Tt])
                dma(ffn[ic], gT[ic * 128:(ic + 1) * 128, c0:c0 + Tt])
            wld()
            for ic in range(16):
                nc.vector.tensor_mul(og_bf[ic], att[ic], ffn[ic])
            nc.all_engine_barrier()
            for oc in range(16):
                for ic in range(16):
                    dma(W2f[ic], WoT[ic * 128:(ic + 1) * 128, oc * 128:(oc + 1) * 128])
                wld()
                for ic in range(16):
                    nc.vector.tensor_copy(W2b[ic], W2f[ic])
                nc.all_engine_barrier()
                for ic in range(16):
                    nc.tensor.matmul(ps_p, W2b[ic], og_bf[ic], start=(ic == 0), stop=(ic == 15))
                nc.all_engine_barrier()
                nc.vector.tensor_copy(att[oc], ps_p)
                nc.all_engine_barrier()
            lnfull(att, r1, None, add_to=r1)
            lnfull(r1, ffn, None, add_to=None)
            for ic in range(16):
                nc.vector.tensor_copy(xn2[ic], ffn[ic])
            nc.all_engine_barrier()
            for m in range(NI):
                for half, W, ssb, pst in [(0, gwT, gs_sb, ps_p), (1, uwT, us_sb, ps_u)]:
                    for ic in range(16):
                        dma(W2f[ic], W[m, ic * 128:(ic + 1) * 128, :])
                    wld()
                    for ic in range(16):
                        nc.vector.tensor_copy(W2b[ic], W2f[ic])
                    nc.all_engine_barrier()
                    for ic in range(16):
                        nc.tensor.matmul(pst, W2b[ic], xn2[ic],
                                         start=(ic == 0), stop=(ic == 15))
                    nc.all_engine_barrier()
                nc.vector.tensor_scalar_mul(tmpc, ps_p, gs_sb[:, m:m + 1])
                nc.all_engine_barrier()
                nc.scalar.activation(tmpc, tmpc, AF.Silu)
                nc.all_engine_barrier()
                nc.vector.tensor_scalar_mul(evac[0], ps_u, us_sb[:, m:m + 1])
                nc.all_engine_barrier()
                nc.vector.tensor_mul(hall[m], tmpc, evac[0])
                nc.all_engine_barrier()
            for oc in range(16):
                for m in range(NI):
                    dma(W2f[m % 16], dwT[m, :, oc * 128:(oc + 1) * 128])
                    wld()
                    nc.vector.tensor_copy(W2b[m % 16], W2f[m % 16])
                    nc.all_engine_barrier()
                    nc.tensor.matmul(ps_p, W2b[m % 16], hall[m],
                                     start=(m == 0), stop=(m == NI - 1))
                    nc.all_engine_barrier()
                nc.vector.tensor_scalar_mul(ffn[oc], ps_p, ds_sb[:, oc:oc + 1])
                nc.all_engine_barrier()
            lnfull(ffn, att, None, add_to=r1)
            if tt > 0:
                nc.vector.wait_ge(st, 16 * nst[0])
                nc.all_engine_barrier()
            for ic in range(16):
                dmao(outT[ic * 128:(ic + 1) * 128, c0:c0 + Tt], att[ic])
            nc.all_engine_barrier()
        nc.vector.wait_ge(st, 16 * nst[0])
        nc.all_engine_barrier()
    for t_ in reversed(ctx):
        t_.__exit__(None, None, None)
    return nc


def kernel(**inp):
    x = np.asarray(inp["L0_attn_x"], np.float32)[0]
    xTf = np.ascontiguousarray(x.T)
    ones_c = np.ones((128, 1), np.float32)
    ones_r = np.ones((1, 128), np.float32)
    ident = np.eye(128, dtype=np.float32)

    def t(a):
        return np.ascontiguousarray(np.asarray(a, np.float32).T)

    WqT, WkT, WvT, WaT, WgT = (t(inp["L0_attn_" + n]) for n in ["Wq", "Wk", "Wv", "Wa", "Wg"])
    WbT = t(inp["L0_attn_Wb"])
    ba_r = np.ascontiguousarray(np.asarray(inp["L0_attn_ba"], np.float32).reshape(16, 128).T)
    bb_r = np.ascontiguousarray(np.asarray(inp["L0_attn_bb"], np.float32).reshape(16, 1))

    ncA = build_A(TPC)
    common = dict(WqT=WqT, WkT=WkT, WvT=WvT, WaT=WaT, WgT=WgT, WbT=WbT,
                  ba_r=ba_r, bb_r=bb_r, ones_c=ones_c, ones_r=ones_r)
    in_maps = [dict(xT=np.ascontiguousarray(xTf[:, c * TPC:(c + 1) * TPC]), **common)
               for c in range(NC)]
    resA = run_bass_kernel_spmd(ncA, in_maps, core_ids=list(range(NC)))
    qpre = np.concatenate([r["qpre"] for r in resA.results], axis=1)
    kpre = np.concatenate([r["kpre"] for r in resA.results], axis=1)
    vpre = np.concatenate([r["vpre"] for r in resA.results], axis=1)
    a_f = np.concatenate([r["a_o"] for r in resA.results], axis=1)
    g_f = np.concatenate([r["g_o"] for r in resA.results], axis=1)
    beta_f = np.concatenate([r["beta_o"] for r in resA.results], axis=1)

    ncB = build_B()
    qcw = np.asarray(inp["L0_attn_qconv_w"], np.float32)
    kcw = np.asarray(inp["L0_attn_kconv_w"], np.float32)
    vcw = np.asarray(inp["L0_attn_vconv_w"], np.float32)
    qcb = np.asarray(inp["L0_attn_qconv_b"], np.float32).reshape(-1, 1)
    kcb = np.asarray(inp["L0_attn_kconv_b"], np.float32).reshape(-1, 1)
    vcb = np.asarray(inp["L0_attn_vconv_b"], np.float32).reshape(-1, 1)
    lnw = np.asarray(inp["L0_attn_ln_w"], np.float32).reshape(128, 1)
    lnb = np.asarray(inp["L0_attn_ln_b"], np.float32).reshape(128, 1)
    in_maps = []
    for c in range(NC):
        s = slice(256 * c, 256 * (c + 1))
        in_maps.append(dict(
            qp=np.ascontiguousarray(np.pad(qpre[s], ((0, 0), (3, 0)))),
            kp=np.ascontiguousarray(np.pad(kpre[s], ((0, 0), (3, 0)))),
            vp=np.ascontiguousarray(np.pad(vpre[s], ((0, 0), (3, 0)))),
            ah=np.ascontiguousarray(a_f[s]),
            bet=np.ascontiguousarray(beta_f[2 * c:2 * c + 2]),
            qcw=np.ascontiguousarray(qcw[s]), kcw=np.ascontiguousarray(kcw[s]),
            vcw=np.ascontiguousarray(vcw[s]),
            qcb=np.ascontiguousarray(qcb[s]), kcb=np.ascontiguousarray(kcb[s]),
            vcb=np.ascontiguousarray(vcb[s]),
            lnw=lnw, lnb=lnb, ident=ident, ones_c=ones_c, ones_r=ones_r))
    resB = run_bass_kernel_spmd(ncB, in_maps, core_ids=list(range(NC)))
    o_f = np.concatenate([r["o_h"] for r in resB.results], axis=0)

    INTER = 5632
    NI = INTER // 128
    ncC = build_C(TPC)
    WoT = t(inp["L0_attn_Wo"])
    gwT = np.ascontiguousarray(t(inp["L0_ffn_gate_w"]).reshape(D, NI, 128).transpose(1, 0, 2))
    uwT = np.ascontiguousarray(t(inp["L0_ffn_up_w"]).reshape(D, NI, 128).transpose(1, 0, 2))
    dwT = np.ascontiguousarray(t(inp["L0_ffn_down_w"]).reshape(NI, 128, D))
    gs_r = np.ascontiguousarray(np.asarray(inp["L0_ffn_gate_s"], np.float32).reshape(NI, 128).T)
    us_r = np.ascontiguousarray(np.asarray(inp["L0_ffn_up_s"], np.float32).reshape(NI, 128).T)
    ds_r = np.ascontiguousarray(np.asarray(inp["L0_ffn_down_s"], np.float32).reshape(16, 128).T)
    common = dict(WoT=WoT, gwT=gwT, uwT=uwT, dwT=dwT, gs_r=gs_r, us_r=us_r,
                  ds_r=ds_r, ones_c=ones_c, ones_r=ones_r)
    in_maps = [dict(xT=np.ascontiguousarray(xTf[:, c * TPC:(c + 1) * TPC]),
                    oT=np.ascontiguousarray(o_f[:, c * TPC:(c + 1) * TPC]),
                    gT=np.ascontiguousarray(g_f[:, c * TPC:(c + 1) * TPC]), **common)
               for c in range(NC)]
    resC = run_bass_kernel_spmd(ncC, in_maps, core_ids=list(range(NC)))
    outT = np.concatenate([r["outT"] for r in resC.results], axis=1)
    return np.ascontiguousarray(outT.T).reshape(1, T, D).astype(np.float32)
